# version 06
# baseline (speedup 1.0000x reference)
import numpy as np
import ml_dtypes
from contextlib import ExitStack

import concourse.bass as bass
import concourse.mybir as mybir
from concourse.bass_utils import run_bass_kernel_spmd

F32 = mybir.dt.float32
BF16 = mybir.dt.bfloat16
AF = mybir.ActivationFunctionType
ALU = mybir.AluOpType
AX = mybir.AxisListType

B, S, DM = 4, 2048, 2048
H, DH = 16, 128
E, FEXP, FSH = 16, 1024, 2048
TOK = 1024
NTT = 16
NOT = 8
NDI = 16
ISQ = float(1.0 / np.sqrt(128.0))
SHIFT = 12.0
EPS = 1e-5

STREAMS = ("tensor", "vector", "scalar", "gpsimd", "sync")
DBG = set()
SKIP = set()
NCORES = 8


class Prog:

    def __init__(self, nc):
        self.nc = nc
        self.items = {e: [] for e in STREAMS}
        self.ndma = {}
        self.last_real = {e: None for e in STREAMS}
        self.res = {}

    def emit(self, eng, fn, reads=(), writes=(), dma=False, dsem=None):
        waits = []

        def need(tok):
            if tok[0] == "c" and tok[1] == eng and not dma and eng == "tensor":
                return
            if tok[0] == "c" and tok[1] == eng and not dma:
                waits.append(tok)
                return
            waits.append(tok)

        for k in reads:
            r = self.res.get(k)
            if r and r["w"] is not None:
                need(r["w"])
        for k in writes:
            r = self.res.get(k)
            if r:
                if r["w"] is not None:
                    need(r["w"])
                for tok in r["r"].values():
                    need(tok)

        idx = len(self.items[eng])
        if dma:
            skey = dsem if dsem is not None else (writes[0] if writes else reads[0])
            n = self.ndma.get(skey, 0)
            tok = ("d", skey, n)
            self.ndma[skey] = n + 1
        else:
            tok = ("c", eng, idx)
            self.last_real[eng] = tok
        self.items[eng].append({"fn": fn, "waits": waits, "dma": dma,
                                "dkey": (skey if dma else None), "inc": dma})
        for k in reads:
            r = self.res.setdefault(k, {"w": None, "r": {}})
            r["r"][tok[0] + str(tok[1])] = tok
        for k in writes:
            self.res[k] = {"w": tok, "r": {}}
        return tok

    def barrier(self):
        toks = []
        for e in STREAMS:
            if self.last_real[e] is not None:
                toks.append(self.last_real[e])
        for skey, n in self.ndma.items():
            toks.append(("d", skey, n - 1))
        for e in STREAMS:
            waits = [t for t in toks if not (t[0] == "c" and t[1] == e)]
            self.items[e].append({"fn": None, "waits": waits, "dma": False,
                                  "dkey": None, "inc": False})
        self.res = {}

    def finalize(self):
        nc = self.nc
        for e in STREAMS:
            for it in self.items[e]:
                for tok in it["waits"]:
                    if tok[0] == "c":
                        self.items[tok[1]][tok[2]]["inc"] = True
        vals = {}
        for e in STREAMS:
            n = 0
            for i, it in enumerate(self.items[e]):
                if it["inc"] and not it["dma"]:
                    n += 1
                    vals[("c", e, i)] = n
        ctx = ExitStack()
        csem, dsem = {}, {}
        for e in STREAMS:
            if any(it["inc"] and not it["dma"] for it in self.items[e]):
                csem[e] = ctx.enter_context(nc.semaphore(f"cs_{e}"))
        for skey in self.ndma:
            dsem[skey] = ctx.enter_context(nc.semaphore(f"ds_{skey}"))

        def wait_spec(tok):
            if tok[0] == "d":
                return ("d" + str(tok[1]), dsem[tok[1]], 16 * (tok[2] + 1))
            return ("c" + tok[1], csem[tok[1]], vals[tok])

        block = ctx.enter_context(nc.Block())

        def body(engname):
            def run(engine):
                hi = {}
                for it in self.items[engname]:
                    for tok in it["waits"]:
                        key, sem, v = wait_spec(tok)
                        if hi.get(key, 0) >= v:
                            continue
                        hi[key] = v
                        engine.wait_ge(sem, v)
                    if it["fn"] is None:
                        continue
                    inst = it["fn"](engine)
                    if it["dma"]:
                        inst.then_inc(dsem[it["dkey"]], 16)
                    elif it["inc"]:
                        inst.then_inc(csem[engname], 1)
            return run

        block.tensor(body("tensor"))
        block.vector(body("vector"))
        block.scalar(body("scalar"))
        block.gpsimd(body("gpsimd"))
        block.sync(body("sync"))
        ctx.close()



def build_program():
    nc = bass.Bass("TRN2", target_bir_lowering=False)
    p = Prog(nc)

    d_xp = nc.dram_tensor("xp", [NTT, 128, DM], F32, kind="ExternalInput").ap()
    d_kadj = nc.dram_tensor("kadj", [128, NTT], F32, kind="ExternalInput").ap()
    d_idb = nc.dram_tensor("idb", [128, 128], BF16, kind="ExternalInput").ap()
    d_idf = nc.dram_tensor("idf", [128, 128], F32, kind="ExternalInput").ap()
    slim = "slim" in SKIP
    def dram_in(name, shape, used=True):
        if slim and not used:
            shape = [1, 1]
        return nc.dram_tensor(name, shape, F32, kind="ExternalInput").ap()
    use_attn = "s1" not in SKIP
    use_s2 = "s2" not in SKIP
    use_s4 = "s4" not in SKIP
    d_wq = dram_in("wq_t", [H, 128, NDI * DH], use_attn)
    d_wk = dram_in("wk_t", [H, 128, NDI * DH], use_attn)
    d_wv = dram_in("wv_t", [H, 128, NDI * DH], use_attn)
    d_ow = dram_in("ow_t", [NDI, 128, DM], use_s2)
    d_rw = dram_in("rw_t", [NDI, 128, E])
    d_sg = dram_in("sg_t", [4, 128, NDI * 512])
    d_su = dram_in("su_t", [4, 128, NDI * 512])
    d_sd = dram_in("sd_t", [4, 128, 16 * 512])
    d_gw = dram_in("gw_t", [E, 2, 128, NDI * 512], use_s4)
    d_uw = dram_in("uw_t", [E, 2, 128, NDI * 512], use_s4)
    d_dw = dram_in("dw_t", [E, 2, 128, 8 * 1024], use_s4)
    d_out = nc.dram_tensor("out", [NOT, 128, DM], F32, kind="ExternalOutput").ap()
    dtap = {}
    for nm, shape in (("xnt", [128, NDI * TOK]), ("attnT", [128, H * TOK]),
                      ("po", [128, NOT * DM]), ("lt", [128, NOT * E]),
                      ("ge", [128, NOT * E]), ("acc", [128, NOT * DM]),
                      ("hsh", [128, 16 * TOK]), ("acc3", [128, NOT * DM]),
                      ("he0", [128, 8 * TOK]), ("m8w", [128, 16]),
                      ("gb0", [128, NDI * 512]), ("gb1", [128, NDI * 512]),
                      ("gss", [128, 4 * TOK]),
                      ("h0KT", [128, S]), ("h0QT", [128, TOK]),
                      ("h0Vh", [128, NTT * DH]), ("h0Pu", [128, NTT * 512]),
                      ("h0rb", [128, 512])):
        if ("h0" in DBG and nm.startswith("h0")) or nm in DBG:
            dtap[nm] = nc.dram_tensor("tap_" + nm, shape, F32, kind="ExternalOutput").ap()

    def tap(nm, ap_fn, reads):
        if nm in dtap:
            p.emit("gpsimd", lambda e, nm=nm: e.dma_start(out=dtap[nm], in_=ap_fn()),
                   reads=reads, dma=True)

    top = ExitStack()
    sb = lambda name, shape, dt: top.enter_context(nc.sbuf_tensor(name, shape, dt))

    idb = sb("sidb", [128, 128], BF16)
    idf = sb("sidf", [128, 128], F32)
    kadj = sb("skadj", [128, NTT], F32)
    qio = sb("sqio", [128, TOK], F32)
    onec = sb("sonec", [128, 1], BF16)
    oner = sb("soner", [1, 128], F32)
    xnT_own = sb("sxnT_own", [128, NDI * TOK], BF16)
    ACC = sb("sACC", [128, NOT * DM], F32)
    lt = sb("slt", [128, NOT * E], F32)
    ge = sb("sge", [128, NOT * E], F32)
    stat = sb("sstat", [128, 16], F32)
    cst = sb("scst", [128, 2], F32)

    if "init" not in SKIP:
        p.emit("sync", lambda e: e.dma_start(out=idb[:], in_=d_idb), writes=["idb"], dma=True)
        p.emit("sync", lambda e: e.dma_start(out=idf[:], in_=d_idf), writes=["idf"], dma=True)
        p.emit("sync", lambda e: e.dma_start(out=kadj[:], in_=d_kadj), writes=["kadj"], dma=True)
    if "init" not in SKIP:
        p.emit("vector", lambda e: e.memset(onec[:], 1.0), writes=["onec"])
        p.emit("vector", lambda e: e.memset(cst[:, 0:1], EPS), writes=["cst"])
        p.emit("vector", lambda e: e.memset(cst[:, 1:2], -SHIFT), writes=["cst"])
        p.emit("vector", lambda e: e.memset(oner[:], 1.0), writes=["oner"])
        p.emit("gpsimd", lambda e: e.iota(qio[:], pattern=[[1, TOK]], base=0,
                                          channel_multiplier=0,
                                          allow_small_or_imprecise_dtypes=True),
               writes=["qio"])

    with ExitStack() as scopeA:
        sba = lambda name, shape, dt: scopeA.enter_context(nc.sbuf_tensor(name, shape, dt))
        attnT = sba("sattnT", [128, H * TOK], BF16)
        scopeB = ExitStack()
        xnT_oth = scopeB.enter_context(nc.sbuf_tensor("sxnT_oth", [128, NDI * TOK], BF16))

        if "s0comp" in SKIP:
            d_xno = nc.dram_tensor("dbg_xnt_own", [128, NDI * TOK], F32, kind="ExternalInput").ap()
            d_xnt2 = nc.dram_tensor("dbg_xnt_oth", [128, NDI * TOK], F32, kind="ExternalInput").ap()
            p.emit("gpsimd", lambda e: e.dma_start(out=xnT_own[:], in_=d_xno),
                   writes=["xnT_own"], dma=True)
            p.emit("gpsimd", lambda e: e.dma_start(out=xnT_oth[:], in_=d_xnt2),
                   writes=["xnT_oth"], dma=True)
        with ExitStack() as s0:
            sb0 = lambda name, shape, dt: s0.enter_context(nc.sbuf_tensor(name, shape, dt))
            xt = [sb0(f"sxt{i}", [128, DM], F32) for i in range(3)]
            sqs = sb0("ssqs", [128, DM], BF16)
            xnt = [sb0(f"sxnt{i}", [128, DM], BF16) for i in range(2)]
            pst = [s0.enter_context(nc.psum_tensor(f"pst{i}", [128, 128], BF16)) for i in range(2)]

            for tt in range(NTT if "s0comp" not in SKIP else 0):
                xbuf = xt[tt % 3]
                kx = f"xt{tt % 3}"
                p.emit("sync", lambda e, tt=tt, xbuf=xbuf: e.dma_start(out=xbuf[:], in_=d_xp[tt]),
                       writes=[kx], dma=True)
                p.emit("vector", lambda e, xbuf=xbuf: e.reduce_sum(stat[:, 0:1], xbuf[:], axis=AX.X),
                       reads=[kx], writes=["st_sum"])
                p.emit("scalar", lambda e, xbuf=xbuf: e.activation(sqs[:], xbuf[:], AF.Square,
                                                                   accum_out=stat[:, 1:2]),
                       reads=[kx], writes=["st_sq", "sqs"])
                p.emit("vector", lambda e: e.tensor_scalar(stat[:, 2:3], stat[:, 0:1],
                                                           1.0 / DM, None, ALU.mult),
                       reads=["st_sum"], writes=["st_mean"])
                p.emit("vector", lambda e: e.tensor_tensor(stat[:, 3:4], stat[:, 2:3],
                                                           stat[:, 2:3], ALU.mult),
                       reads=["st_mean"], writes=["st_m2"])
                p.emit("vector", lambda e: e.tensor_scalar(stat[:, 4:5], stat[:, 1:2],
                                                           1.0 / DM, stat[:, 3:4],
                                                           ALU.mult, ALU.subtract),
                       reads=["st_sq", "st_m2"], writes=["st_var"])
                p.emit("scalar", lambda e: e.activation(stat[:, 5:6], stat[:, 4:5],
                                                        AF.Sqrt, bias=cst[:, 0:1]),
                       reads=["st_var", "cst"], writes=["st_std"])
                p.emit("vector", lambda e: e.reciprocal(stat[:, 6:7], stat[:, 5:6]),
                       reads=["st_std"], writes=["st_rstd"])
                xnb = xnt[tt % 2]
                kxn = f"xnt{tt % 2}"
                p.emit("vector", lambda e, xbuf=xbuf, xnb=xnb: e.tensor_scalar(
                    xnb[:], xbuf[:], stat[:, 2:3], stat[:, 6:7], ALU.subtract, ALU.mult),
                    reads=[kx, "st_mean", "st_rstd"], writes=[kxn])
                dst = xnT_own if tt < NOT else xnT_oth
                dkey = "xnT_own" if tt < NOT else "xnT_oth"
                co = (tt % NOT) * 128
                for dint in range(NDI):
                    pb = pst[dint % 2]
                    kpb = f"pst{dint % 2}"
                    p.emit("tensor", lambda e, xnb=xnb, dint=dint, pb=pb: e.transpose(
                        pb[:], xnb[:, dint * 128:(dint + 1) * 128], idb[:]),
                        reads=[kxn, "idb"], writes=[kpb])
                    p.emit("scalar", lambda e, pb=pb, dst=dst, dint=dint, co=co: e.copy(
                        dst[:, dint * TOK + co: dint * TOK + co + 128], pb[:]),
                        reads=[kpb], writes=[dkey])
        tap("xnt", lambda: xnT_own[:], ["xnT_own"])
        p.barrier()

        with ExitStack() as s1:
            sb1 = lambda name, shape, dt: s1.enter_context(nc.sbuf_tensor(name, shape, dt))
            wb = [sb1(f"swb{i}", [128, NDI * DH], BF16) for i in range(2)]
            KT = sb1("sKT", [128, S], BF16)
            QT = sb1("sQT", [128, TOK], BF16)
            VT = sb1("sVT", [128, 512], BF16)
            Vh = sb1("sVh", [128, NTT * DH], BF16)
            Pu = sb1("sPu", [128, NTT * 512], BF16)
            rb = sb1("srb", [128, 512], F32)
            rinv = sb1("srinv", [1, 512], F32)
            psA = s1.enter_context(nc.psum_tensor("psA", [128, 512], F32))
            psB = s1.enter_context(nc.psum_tensor("psB", [128, 512], F32))
            psC = s1.enter_context(nc.psum_tensor("psC", [128, 512], F32))
            psD = s1.enter_context(nc.psum_tensor("psD", [128, 512], F32))
            psDen = s1.enter_context(nc.psum_tensor("psDen", [128, 512], F32))
            psT = [s1.enter_context(nc.psum_tensor(f"psT{i}", [128, 128], BF16))
                   for i in range(2)]
            psE = s1.enter_context(nc.psum_tensor("psE", [128, 512], F32))

            chunks = []
            for h in range(H):
                chunks += [(d_wk, h), (d_wv, h), (d_wq, h)]

            def emit_load(i):
                dram, h = chunks[i]
                slot = i % 2
                p.emit("gpsimd", lambda e, b=wb[slot], dram=dram, h=h: e.dma_start(
                    out=b[:], in_=dram[h]), writes=[f"wb{slot}"], dma=True)

            if "s1" not in SKIP:
                emit_load(0); emit_load(1)
            for h in range(H if "s1" not in SKIP else 0):
                ks, vs, qs = (3 * h) % 2, (3 * h + 1) % 2, (3 * h + 2) % 2
                for grp in range(4):
                    src = xnT_own if grp < 2 else xnT_oth
                    skey = "xnT_own" if grp < 2 else "xnT_oth"
                    off = (grp % 2) * 512
                    for dint in range(NDI):
                        p.emit("tensor", lambda e, dint=dint, src=src, off=off, ks=ks: e.matmul(
                            psA[:], wb[ks][:, dint * DH:(dint + 1) * DH],
                            src[:, dint * TOK + off: dint * TOK + off + 512],
                            start=(dint == 0), stop=(dint == NDI - 1)),
                            reads=[f"wb{ks}", skey], writes=["psA"])
                    p.emit("scalar", lambda e, grp=grp: e.copy(
                        KT[:, grp * 512:(grp + 1) * 512], psA[:]),
                        reads=["psA"], writes=["KT"])
                if 3 * h + 2 < len(chunks):
                    emit_load(3 * h + 2)
                for grp in range(4):
                    src = xnT_own if grp < 2 else xnT_oth
                    skey = "xnT_own" if grp < 2 else "xnT_oth"
                    off = (grp % 2) * 512
                    for dint in range(NDI):
                        p.emit("tensor", lambda e, dint=dint, src=src, off=off, vs=vs: e.matmul(
                            psB[:], wb[vs][:, dint * DH:(dint + 1) * DH],
                            src[:, dint * TOK + off: dint * TOK + off + 512],
                            start=(dint == 0), stop=(dint == NDI - 1)),
                            reads=[f"wb{vs}", skey], writes=["psB"])
                    p.emit("scalar", lambda e: e.copy(VT[:], psB[:]),
                           reads=["psB"], writes=["VT"])
                    for j in range(4):
                        kt = grp * 4 + j
                        pt = psT[j % 2]
                        kpt = f"psT{j % 2}"
                        p.emit("tensor", lambda e, j=j, pt=pt: e.transpose(
                            pt[:], VT[:, j * 128:(j + 1) * 128], idb[:]),
                            reads=["VT", "idb"], writes=[kpt])
                        p.emit("scalar", lambda e, kt=kt, pt=pt: e.copy(
                            Vh[:, kt * DH:(kt + 1) * DH], pt[:]),
                            reads=[kpt], writes=["Vh"])
                if 3 * h + 3 < len(chunks):
                    emit_load(3 * h + 3)
                for grp in range(2):
                    off = grp * 512
                    for dint in range(NDI):
                        p.emit("tensor", lambda e, dint=dint, off=off, qs=qs: e.matmul(
                            psC[:], wb[qs][:, dint * DH:(dint + 1) * DH],
                            xnT_own[:, dint * TOK + off: dint * TOK + off + 512],
                            start=(dint == 0), stop=(dint == NDI - 1)),
                            reads=[f"wb{qs}", "xnT_own"], writes=["psC"])
                    p.emit("scalar", lambda e, off=off: e.copy(
                        QT[:, off:off + 512], psC[:]),
                        reads=["psC"], writes=["QT"])
                if 3 * h + 4 < len(chunks):
                    emit_load(3 * h + 4)

                for qg in range(2):
                    qoff = qg * 512
                    psS = [(psA, "psA"), (psB, "psB"), (psE, "psE")]
                    for kt in range(NTT):
                        pk, kps = psS[kt % 3]
                        p.emit("tensor", lambda e, kt=kt, pk=pk, qoff=qoff: e.matmul(
                            pk[:], KT[:, kt * 128:(kt + 1) * 128],
                            QT[:, qoff:qoff + 512], start=True, stop=True),
                            reads=["KT", "QT"], writes=[kps])
                        p.emit("scalar", lambda e, kt=kt, pk=pk: e.activation(
                            Pu[:, kt * 512:(kt + 1) * 512], pk[:], AF.Exp,
                            bias=cst[:, 1:2], scale=ISQ),
                            reads=[kps, "cst"], writes=[f"pu{kt}"])
                        p.emit("vector", lambda e, kt=kt, qoff=qoff: e.scalar_tensor_tensor(
                            Pu[:, kt * 512:(kt + 1) * 512],
                            qio[:, qoff:qoff + 512], kadj[:, kt:kt + 1],
                            Pu[:, kt * 512:(kt + 1) * 512],
                            ALU.is_ge, ALU.mult),
                            reads=[f"pu{kt}", "qio", "kadj"], writes=[f"pu{kt}"])
                    for kt in range(NTT):
                        p.emit("tensor", lambda e, kt=kt: e.matmul(
                            psDen[0:1, :], onec[:], Pu[:, kt * 512:(kt + 1) * 512],
                            start=(kt == 0), stop=(kt == NTT - 1)),
                            reads=[f"pu{kt}", "onec"], writes=["psDen"])
                    for kt in range(NTT):
                        p.emit("tensor", lambda e, kt=kt: e.matmul(
                            psC[:], Vh[:, kt * DH:(kt + 1) * DH],
                            Pu[:, kt * 512:(kt + 1) * 512],
                            start=(kt == 0), stop=(kt == NTT - 1)),
                            reads=[f"pu{kt}", "Vh"], writes=["psC"])
                    p.emit("vector", lambda e: e.reciprocal(rinv[:], psDen[0:1, :]),
                           reads=["psDen"], writes=["rinv"])
                    p.emit("tensor", lambda e: e.matmul(
                        psD[:], oner[:], rinv[:], start=True, stop=True),
                        reads=["rinv", "oner"], writes=["psD"])
                    p.emit("scalar", lambda e: e.copy(rb[:], psD[:]),
                           reads=["psD"], writes=["rb"])
                    p.emit("vector", lambda e, h=h, qoff=qoff: e.tensor_tensor(
                        attnT[:, h * TOK + qoff: h * TOK + qoff + 512],
                        psC[:], rb[:], ALU.mult),
                        reads=["psC", "rb"], writes=["attnT"])
                    if h == 0 and qg == 0 and "h0" in DBG:
                        tap("h0KT", lambda: KT[:], ["KT"])
                        tap("h0QT", lambda: QT[:], ["QT"])
                        tap("h0Vh", lambda: Vh[:], ["Vh"])
                        tap("h0Pu", lambda: Pu[:], ["pu%d" % k2 for k2 in range(NTT)])
                        tap("h0rb", lambda: rb[:], ["rb"])
        tap("attnT", lambda: attnT[:], ["attnT"])
        p.barrier()
        scopeB.close()

        with ExitStack() as s2:
            sb2 = lambda name, shape, dt: s2.enter_context(nc.sbuf_tensor(name, shape, dt))
            owb = sb2("sowb", [128, H * DM], BF16)
            psP = [s2.enter_context(nc.psum_tensor(f"psP{i}", [128, 512], F32))
                   for i in range(8)]
            if "s2" in SKIP and "noaccm" not in SKIP:
                p.emit("vector", lambda e: e.memset(ACC[:], 0.0), writes=["ACC"])
            for j in range(4 if "s2" not in SKIP else 0):
                p.emit("gpsimd", lambda e, j=j: e.dma_start(
                    out=owb[:, j * 4 * DM:(j + 1) * 4 * DM].rearrange(
                        "p (t d) -> p t d", t=4),
                    in_=d_ow.rearrange("t p d -> p t d")[:, j * 4:(j + 1) * 4, :]),
                    writes=[f"owb{j}"], dma=True)
            for tp in range(4 if "s2" not in SKIP else 0):
                for ft in range(H):
                    for t2 in range(2):
                        tt = tp * 2 + t2
                        for db in range(4):
                            pi = t2 * 4 + db
                            p.emit("tensor", lambda e, ft=ft, tt=tt, db=db, pi=pi: e.matmul(
                                psP[pi][:],
                                attnT[:, ft * TOK + tt * 128: ft * TOK + tt * 128 + 128],
                                owb[:, ft * DM + db * 512: ft * DM + db * 512 + 512],
                                start=(ft == 0), stop=(ft == H - 1)),
                                reads=["attnT", f"owb{ft // 4}"], writes=[f"psP{pi}"])
                for t2 in range(2):
                    tt = tp * 2 + t2
                    for db in range(4):
                        pi = t2 * 4 + db
                        p.emit("scalar", lambda e, tt=tt, db=db, pi=pi: e.copy(
                            ACC[:, tt * DM + db * 512: tt * DM + db * 512 + 512],
                            psP[pi][:]),
                            reads=[f"psP{pi}"], writes=["ACC"])
        tap("po", lambda: ACC[:], ["ACC"])
        p.barrier()

    with ExitStack() as s3:
        sb3 = lambda name, shape, dt: s3.enter_context(nc.sbuf_tensor(name, shape, dt))
        h_sh = sb3("sh_sh", [128, 16 * TOK], BF16)
        with ExitStack() as s3a:
            sb3a = lambda name, shape, dt: s3a.enter_context(nc.sbuf_tensor(name, shape, dt))
            rwb = sb3a("srwb", [128, NDI * E], BF16)
            ltT = sb3a("sltT", [16, TOK], F32)
            m8 = sb3a("sm8", [128, 8], F32)
            d12 = sb3a("sd12", [128, 2], F32)
            w12 = sb3a("sw12", [128, 2], F32)
            eqs = sb3a("seqs", [128, 2 * E], F32)
            gub = [sb3a(f"sgub{i}", [128, NDI * 512], BF16) for i in range(2)]
            gs = sb3a("sgs", [128, 4 * TOK], BF16)
            psg = [s3a.enter_context(nc.psum_tensor(f"psg{i}", [128, 512], F32))
                   for i in range(8)]

            if {"s3r", "s3g", "s3x1", "s3x2"} & SKIP:
                p.emit("vector", lambda e: e.memset(ge[:], 0.0625), writes=["ge"])
            else:
                p.emit("gpsimd", lambda e: e.dma_start(
                    out=rwb.rearrange("p (t e) -> p t e", t=NDI),
                    in_=d_rw.rearrange("t p e -> p t e")),
                    writes=["rwb"], dma=True)
            for tt in range(NOT if not ({"s3r", "s3x1"} & SKIP) else 0):
                for dint in range(NDI):
                    p.emit("tensor", lambda e, tt=tt, dint=dint, pt=psg: e.matmul(
                        pt[0][:, 0:E],
                        xnT_own[:, dint * TOK + tt * 128: dint * TOK + tt * 128 + 128],
                        rwb[:, dint * E:(dint + 1) * E],
                        start=(dint == 0), stop=(dint == NDI - 1)),
                        reads=["rwb", "xnT_own"], writes=["psg0"])
                p.emit("vector", lambda e, tt=tt, pt=psg: e.tensor_copy(
                    lt[:, tt * E:(tt + 1) * E], pt[0][:, 0:E]),
                    reads=["psg0"], writes=["lt"])
            for tt in range(NOT if not ({"s3r", "s3x1"} & SKIP) else 0):
                if "s3g" in SKIP:
                    continue
                p.emit("vector", lambda e, tt=tt: e.max(
                    out=m8[:], in_=lt[:, tt * E:(tt + 1) * E]),
                    reads=["lt"], writes=["m8"])
                p.emit("vector", lambda e: e.tensor_sub(d12[:, 0:1], m8[:, 0:1], m8[:, 1:2]),
                       reads=["m8"], writes=["d12"])
                p.emit("vector", lambda e: e.tensor_sub(d12[:, 1:2], m8[:, 1:2], m8[:, 0:1]),
                       reads=["m8"], writes=["d12"])
                p.emit("scalar", lambda e: e.activation(w12[:], d12[:], AF.Sigmoid),
                       reads=["d12"], writes=["w12"])
                p.emit("vector", lambda e, tt=tt: e.tensor_scalar(
                    eqs[:, 0:E], lt[:, tt * E:(tt + 1) * E], m8[:, 0:1], None, ALU.is_equal),
                    reads=["lt", "m8"], writes=["eq1"])
                p.emit("vector", lambda e, tt=tt: e.tensor_scalar(
                    eqs[:, E:2 * E], lt[:, tt * E:(tt + 1) * E], m8[:, 1:2], None, ALU.is_equal),
                    reads=["lt", "m8"], writes=["eq2"])
                p.emit("vector", lambda e: e.tensor_scalar(
                    eqs[:, 0:E], eqs[:, 0:E], w12[:, 0:1], None, ALU.mult),
                    reads=["eq1", "w12"], writes=["eq1"])
                p.emit("vector", lambda e: e.tensor_scalar(
                    eqs[:, E:2 * E], eqs[:, E:2 * E], w12[:, 1:2], None, ALU.mult),
                    reads=["eq2", "w12"], writes=["eq2"])
                p.emit("vector", lambda e, tt=tt: e.tensor_add(
                    ge[:, tt * E:(tt + 1) * E], eqs[:, 0:E], eqs[:, E:2 * E]),
                    reads=["eq1", "eq2"], writes=["ge"])
                if tt == 0 and "m8w" in dtap:
                    p.emit("vector", lambda e: e.tensor_copy(stat[:, 0:8], m8[:]),
                           reads=["m8"], writes=["m8snap"])
                    p.emit("vector", lambda e: e.tensor_copy(stat[:, 8:10], d12[:]),
                           reads=["d12"], writes=["m8snap"])
                    p.emit("vector", lambda e: e.tensor_copy(stat[:, 10:12], w12[:]),
                           reads=["w12"], writes=["m8snap"])
                    p.emit("vector", lambda e: e.tensor_copy(stat[:, 12:14], eqs[:, 0:2]),
                           reads=["eq1"], writes=["m8snap"])
                    tap("m8w", lambda: stat[:], ["m8snap"])

            def gu_block(d_g, d_u, nf, hbuf, hkey, gs_buf, gs_key, gukey, gub, psg):
                order = []
                for c in range(nf):
                    order += [("g", c), ("u", c)]

                def load(i):
                    kind, c = order[i]
                    dram = d_g if kind == "g" else d_u
                    slot = i % 2
                    p.emit("gpsimd", lambda e, b=gub[slot], dram=dram, c=c: e.dma_start(
                        out=b[:], in_=dram[c]), writes=[f"{gukey}{slot}"], dma=True)

                load(0); load(1)
                for i, (kind, c) in enumerate(order):
                    slot = i % 2
                    for fl in range(4):
                        for qg in range(2):
                            pi = fl * 2 + qg
                            for dint in range(NDI):
                                p.emit("tensor", lambda e, dint=dint, fl=fl, qg=qg, pi=pi, slot=slot,
                                       pt=psg, gb=gub: e.matmul(
                                    pt[pi][:],
                                    gb[slot][:, dint * 512 + fl * 128: dint * 512 + fl * 128 + 128],
                                    xnT_own[:, dint * TOK + qg * 512: dint * TOK + qg * 512 + 512],
                                    start=(dint == 0), stop=(dint == NDI - 1)),
                                    reads=[f"{gukey}{slot}", "xnT_own"], writes=[f"psg{pi}"])
                    if i + 2 < len(order):
                        load(i + 2)
                    if kind == "g":
                        for fl in range(4):
                            for qg in range(2):
                                pi = fl * 2 + qg
                                p.emit("scalar", lambda e, fl=fl, qg=qg, pi=pi, pt=psg: e.activation(
                                    gs_buf[:, fl * TOK + qg * 512: fl * TOK + qg * 512 + 512],
                                    pt[pi][:], AF.Silu),
                                    reads=[f"psg{pi}"], writes=[gs_key])
                    else:
                        for fl in range(4):
                            ft = c * 4 + fl
                            for qg in range(2):
                                pi = fl * 2 + qg
                                p.emit("vector", lambda e, fl=fl, ft=ft, qg=qg, pi=pi, pt=psg: e.tensor_tensor(
                                    hbuf[:, ft * TOK + qg * 512: ft * TOK + qg * 512 + 512],
                                    pt[pi][:],
                                    gs_buf[:, fl * TOK + qg * 512: fl * TOK + qg * 512 + 512],
                                    ALU.mult),
                                    reads=[f"psg{pi}", gs_key], writes=[hkey])

            import os as _os
            if _os.environ.get("K_E4"):
                p.barrier()
            if "s3sh" not in SKIP:
                gu_block(d_sg, d_su, 4, h_sh, "h_sh", gs, "gs", "gub", gub, psg)
                tap("gb0", lambda: gub[0][:], ["gub0"])
                tap("gb1", lambda: gub[1][:], ["gub1"])
                tap("gss", lambda: gs[:], ["gs"])
        p.barrier()
        with ExitStack() as s3b:
            sb3b = lambda name, shape, dt: s3b.enter_context(nc.sbuf_tensor(name, shape, dt))
            sdb = [sb3b(f"ssdb{i}", [128, 16 * 512], BF16) for i in range(2)]
            psq = [s3b.enter_context(nc.psum_tensor(f"psq{i}", [128, 512], F32))
                   for i in range(2)]
            for dq in range(4 if not ({"s3sh", "s3d"} & SKIP) else 0):
                slot = dq % 2
                p.emit("gpsimd", lambda e, b=sdb[slot], dq=dq: e.dma_start(
                    out=b[:], in_=d_sd[dq]), writes=[f"sdb{slot}"], dma=True)
                for tt in range(NOT):
                    pb = psq[tt % 2]
                    kpb = f"psq{tt % 2}"
                    for fd in range(16):
                        p.emit("tensor", lambda e, fd=fd, tt=tt, pb=pb, slot=slot: e.matmul(
                            pb[:], h_sh[:, fd * TOK + tt * 128: fd * TOK + tt * 128 + 128],
                            sdb[slot][:, fd * 512:(fd + 1) * 512],
                            start=(fd == 0), stop=(fd == 15)),
                            reads=["h_sh", f"sdb{slot}"], writes=[kpb])
                    p.emit("vector", lambda e, tt=tt, dq=dq, pb=pb: e.tensor_add(
                        ACC[:, tt * DM + dq * 512: tt * DM + dq * 512 + 512],
                        pb[:],
                        ACC[:, tt * DM + dq * 512: tt * DM + dq * 512 + 512]),
                        reads=[kpb, "ACC"], writes=["ACC"])
        tap("hsh", lambda: h_sh[:], ["h_sh"])
        tap("acc3", lambda: ACC[:], ["ACC"])
    p.barrier()

    with ExitStack() as s4:
        sb4 = lambda name, shape, dt: s4.enter_context(nc.sbuf_tensor(name, shape, dt))
        h_e = sb4("sh_e", [128, 8 * TOK], BF16)
        gub4 = [sb4(f"sgub4{i}", [128, NDI * 512], BF16) for i in range(2)]
        gs4 = sb4("sgs4", [128, 4 * TOK], BF16)
        dwb = [sb4(f"sdwb{i}", [128, 8 * 1024], BF16) for i in range(2)]
        psg4 = [s4.enter_context(nc.psum_tensor(f"psg4{i}", [128, 512], F32))
               for i in range(8)]

        for ex in range(E if "s4" not in SKIP else 0):
            order = [("g", 0), ("u", 0), ("g", 1), ("u", 1)]

            def load(i, ex=ex):
                kind, c = order[i]
                dram = d_gw if kind == "g" else d_uw
                slot = i % 2
                p.emit("gpsimd", lambda e, b=gub4[slot], dram=dram, ex=ex, c=c: e.dma_start(
                    out=b[:], in_=dram[ex, c]), writes=[f"gub4{slot}"], dma=True)

            load(0); load(1)
            for dh in range(2):
                p.emit("gpsimd", lambda e, b=dwb[dh], ex=ex, dh=dh: e.dma_start(
                    out=b[:], in_=d_dw[ex, dh]), writes=[f"dwb{dh}"], dma=True)
            for i, (kind, c) in enumerate(order):
                slot = i % 2
                for fl in range(4):
                    for qg in range(2):
                        pi = fl * 2 + qg
                        for dint in range(NDI):
                            p.emit("tensor", lambda e, dint=dint, fl=fl, qg=qg, pi=pi, slot=slot: e.matmul(
                                psg4[pi][:],
                                gub4[slot][:, dint * 512 + fl * 128: dint * 512 + fl * 128 + 128],
                                xnT_own[:, dint * TOK + qg * 512: dint * TOK + qg * 512 + 512],
                                start=(dint == 0), stop=(dint == NDI - 1)),
                                reads=[f"gub4{slot}", "xnT_own"], writes=[f"psg4{pi}"])
                if i + 2 < len(order):
                    load(i + 2)
                if kind == "g":
                    for fl in range(4):
                        for qg in range(2):
                            pi = fl * 2 + qg
                            p.emit("scalar", lambda e, fl=fl, qg=qg, pi=pi: e.activation(
                                gs4[:, fl * TOK + qg * 512: fl * TOK + qg * 512 + 512],
                                psg4[pi][:], AF.Silu),
                                reads=[f"psg4{pi}"], writes=["gs4"])
                else:
                    for fl in range(4):
                        ft = c * 4 + fl
                        for qg in range(2):
                            pi = fl * 2 + qg
                            p.emit("vector", lambda e, fl=fl, ft=ft, qg=qg, pi=pi: e.tensor_tensor(
                                h_e[:, ft * TOK + qg * 512: ft * TOK + qg * 512 + 512],
                                psg4[pi][:],
                                gs4[:, fl * TOK + qg * 512: fl * TOK + qg * 512 + 512],
                                ALU.mult),
                                reads=[f"psg4{pi}", "gs4"], writes=["h_e"])
            if ex == 0:
                tap("he0", lambda: h_e[:], ["h_e"])
            for dh in range(2):
                for tt in range(NOT):
                    for db in range(2):
                        pi = (tt % 4) * 2 + db
                        for fd in range(8):
                            p.emit("tensor", lambda e, fd=fd, tt=tt, db=db, pi=pi, dh=dh: e.matmul(
                                psg4[pi][:],
                                h_e[:, fd * TOK + tt * 128: fd * TOK + tt * 128 + 128],
                                dwb[dh][:, fd * 1024 + db * 512: fd * 1024 + db * 512 + 512],
                                start=(fd == 0), stop=(fd == 7)),
                                reads=["h_e", f"dwb{dh}"], writes=[f"psg4{pi}"])
                        do = dh * 1024 + db * 512
                        p.emit("vector", lambda e, tt=tt, do=do, pi=pi, ex=ex: e.scalar_tensor_tensor(
                            ACC[:, tt * DM + do: tt * DM + do + 512],
                            psg4[pi][:], ge[:, tt * E + ex: tt * E + ex + 1],
                            ACC[:, tt * DM + do: tt * DM + do + 512],
                            ALU.mult, ALU.add),
                            reads=[f"psg4{pi}", "ge", "ACC"], writes=["ACC"])
    tap("lt", lambda: lt[:], ["lt"])
    if "m8w" in dtap:
        pass
    tap("ge", lambda: ge[:], ["ge"])
    tap("acc", lambda: ACC[:], ["ACC"])
    p.barrier()

    with ExitStack() as s5:
        sb5 = lambda name, shape, dt: s5.enter_context(nc.sbuf_tensor(name, shape, dt))
        xb = [sb5(f"sxb5{i}", [128, DM], F32) for i in range(2)]
        ob = [sb5(f"sob5{i}", [128, DM], F32) for i in range(2)]
        for tt in range(NOT if "s5" not in SKIP else 0):
            slot = tt % 2
            p.emit("sync", lambda e, tt=tt, slot=slot: e.dma_start(
                out=xb[slot][:], in_=d_xp[tt]), writes=[f"xb5{slot}"], dma=True)
            p.emit("vector", lambda e, tt=tt, slot=slot: e.tensor_add(
                ob[slot][:], xb[slot][:], ACC[:, tt * DM:(tt + 1) * DM]),
                reads=[f"xb5{slot}", "ACC"], writes=[f"ob5{slot}"])
            p.emit("sync", lambda e, tt=tt, slot=slot: e.dma_start(
                out=d_out[tt], in_=ob[slot][:]), reads=[f"ob5{slot}"], dma=True)

    p.barrier()
    p.finalize()
    top.close()
    return nc



def _pretile(w):
    out = {}
    for nm, key in (("q", "wq_t"), ("k", "wk_t"), ("v", "wv_t")):
        a = w[nm].reshape(NDI, 128, H, DH).transpose(2, 1, 0, 3).reshape(H, 128, NDI * DH)
        out[key] = np.ascontiguousarray(a)
    out["ow_t"] = np.ascontiguousarray(w["o"].reshape(NDI, 128, DM))
    out["rw_t"] = np.ascontiguousarray(w["r"].T.reshape(NDI, 128, E))
    out["sg_t"] = np.ascontiguousarray(
        w["sg"].reshape(NDI, 128, 4, 512).transpose(2, 1, 0, 3).reshape(4, 128, NDI * 512))
    out["su_t"] = np.ascontiguousarray(
        w["su"].reshape(NDI, 128, 4, 512).transpose(2, 1, 0, 3).reshape(4, 128, NDI * 512))
    out["sd_t"] = np.ascontiguousarray(
        w["sd"].reshape(16, 128, 4, 512).transpose(2, 1, 0, 3).reshape(4, 128, 16 * 512))
    out["gw_t"] = np.ascontiguousarray(
        w["gw"].reshape(E, NDI, 128, 2, 512).transpose(0, 3, 2, 1, 4).reshape(E, 2, 128, NDI * 512))
    out["uw_t"] = np.ascontiguousarray(
        w["uw"].reshape(E, NDI, 128, 2, 512).transpose(0, 3, 2, 1, 4).reshape(E, 2, 128, NDI * 512))
    out["dw_t"] = np.ascontiguousarray(
        w["dw"].reshape(E, 8, 128, 2, 1024).transpose(0, 3, 2, 1, 4).reshape(E, 2, 128, 8 * 1024))
    return out


def kernel(L0_attn_hidden_states, L0_attn_q_wei, L0_attn_k_wei, L0_attn_v_wei,
           L0_attn_out_wei, L0_ffn_router_w, L0_ffn_gate_w, L0_ffn_up_w,
           L0_ffn_down_w, L0_ffn_shared_gate_w, L0_ffn_shared_up_w,
           L0_ffn_shared_down_w):
    x = np.asarray(L0_attn_hidden_states, dtype=np.float32)
    wt = _pretile({
        "q": np.asarray(L0_attn_q_wei, np.float32),
        "k": np.asarray(L0_attn_k_wei, np.float32),
        "v": np.asarray(L0_attn_v_wei, np.float32),
        "o": np.asarray(L0_attn_out_wei, np.float32),
        "r": np.asarray(L0_ffn_router_w, np.float32),
        "sg": np.asarray(L0_ffn_shared_gate_w, np.float32),
        "su": np.asarray(L0_ffn_shared_up_w, np.float32),
        "sd": np.asarray(L0_ffn_shared_down_w, np.float32),
        "gw": np.asarray(L0_ffn_gate_w, np.float32),
        "uw": np.asarray(L0_ffn_up_w, np.float32),
        "dw": np.asarray(L0_ffn_down_w, np.float32),
    })
    idb = np.eye(128, dtype=ml_dtypes.bfloat16)
    idf = np.eye(128, dtype=np.float32)

    in_maps = []
    for c in range(8):
        b, par = c // 2, c % 2
        if par == 0:
            perm = np.arange(S)
        else:
            perm = np.concatenate([np.arange(1024, 2048), np.arange(0, 1024)])
        xp = np.ascontiguousarray(x[b][perm]).reshape(NTT, 128, DM)
        own_pos = perm[:TOK]
        kadj = np.searchsorted(own_pos, perm).astype(np.float32)
        kadj_sb = np.ascontiguousarray(kadj.reshape(NTT, 128).T)
        m = {"xp": xp, "kadj": kadj_sb, "idb": idb, "idf": idf}
        if "slim" in SKIP:
            dummy = np.zeros((1, 1), np.float32)
            for nm, used in (("wq_t", "s1" not in SKIP), ("wk_t", "s1" not in SKIP),
                             ("wv_t", "s1" not in SKIP), ("ow_t", "s2" not in SKIP),
                             ("gw_t", "s4" not in SKIP), ("uw_t", "s4" not in SKIP),
                             ("dw_t", "s4" not in SKIP)):
                if not used:
                    m[nm] = dummy
        if "s0comp" in SKIP:
            xpf = xp.reshape(S, DM).astype(np.float64)
            mu = xpf.mean(axis=1, keepdims=True)
            va = xpf.var(axis=1, keepdims=True)
            xnf = ((xpf - mu) / np.sqrt(va + EPS)).astype(np.float32)
            xnt_o = np.ascontiguousarray(
                xnf[:TOK].T.reshape(NDI, 128, TOK).reshape(128 * NDI, TOK)).reshape(NDI, 128, TOK)
            m["dbg_xnt_own"] = np.ascontiguousarray(
                xnt_o.transpose(1, 0, 2).reshape(128, NDI * TOK))
            xnt_t = xnf[TOK:].T.reshape(NDI, 128, TOK)
            m["dbg_xnt_oth"] = np.ascontiguousarray(
                xnt_t.transpose(1, 0, 2).reshape(128, NDI * TOK))
        m.update({k: v for k, v in wt.items() if k not in m})
        in_maps.append(m)

    nc = build_program()
    res = run_bass_kernel_spmd(nc, in_maps[:NCORES], core_ids=list(range(NCORES)))
    if DBG:
        global LAST_RES
        LAST_RES = res

    out = np.empty((B, S, DM), dtype=np.float32)
    for c in range(NCORES):
        b, par = c // 2, c % 2
        o = np.asarray(res.results[c]["out"]).reshape(TOK, DM)
        if par == 0:
            out[b, 0:1024] = o
        else:
            out[b, 1024:2048] = o
    return out


# version 07 (final)
# speedup vs baseline: 1.1136x; 1.1136x over previous
import numpy as np
import ml_dtypes
from contextlib import ExitStack

import concourse.bass as bass
import concourse.mybir as mybir
from concourse.bass_utils import run_bass_kernel_spmd

F32 = mybir.dt.float32
BF16 = mybir.dt.bfloat16
AF = mybir.ActivationFunctionType
ALU = mybir.AluOpType
AX = mybir.AxisListType

B, S, DM = 4, 2048, 2048
H, DH = 16, 128
E, FEXP, FSH = 16, 1024, 2048
TOK = 1024
NTT = 16
NOT = 8
NDI = 16
ISQ = float(1.0 / np.sqrt(128.0))
SHIFT = 12.0
EPS = 1e-5

STREAMS = ("tensor", "vector", "scalar", "gpsimd", "sync")
DBG = set()
SKIP = set()
NCORES = 8


class Prog:

    def __init__(self, nc):
        self.nc = nc
        self.items = {e: [] for e in STREAMS}
        self.ndma = {}
        self.last_real = {e: None for e in STREAMS}
        self.res = {}

    def emit(self, eng, fn, reads=(), writes=(), dma=False, dsem=None):
        waits = []

        def need(tok):
            if tok[0] == "c" and tok[1] == eng and not dma and eng == "tensor":
                return
            if tok[0] == "c" and tok[1] == eng and not dma:
                waits.append(tok)
                return
            waits.append(tok)

        for k in reads:
            r = self.res.get(k)
            if r and r["w"] is not None:
                need(r["w"])
        for k in writes:
            r = self.res.get(k)
            if r:
                if r["w"] is not None:
                    need(r["w"])
                for tok in r["r"].values():
                    need(tok)

        idx = len(self.items[eng])
        if dma:
            skey = dsem if dsem is not None else (writes[0] if writes else reads[0])
            n = self.ndma.get(skey, 0)
            tok = ("d", skey, n)
            self.ndma[skey] = n + 1
        else:
            tok = ("c", eng, idx)
            self.last_real[eng] = tok
        self.items[eng].append({"fn": fn, "waits": waits, "dma": dma,
                                "dkey": (skey if dma else None), "inc": dma})
        for k in reads:
            r = self.res.setdefault(k, {"w": None, "r": {}})
            r["r"][tok[0] + str(tok[1])] = tok
        for k in writes:
            self.res[k] = {"w": tok, "r": {}}
        return tok

    def barrier(self):
        toks = []
        for e in STREAMS:
            if self.last_real[e] is not None:
                toks.append(self.last_real[e])
        for skey, n in self.ndma.items():
            toks.append(("d", skey, n - 1))
        for e in STREAMS:
            waits = [t for t in toks if not (t[0] == "c" and t[1] == e)]
            self.items[e].append({"fn": None, "waits": waits, "dma": False,
                                  "dkey": None, "inc": False})
        self.res = {}

    def finalize(self):
        nc = self.nc
        for e in STREAMS:
            for it in self.items[e]:
                for tok in it["waits"]:
                    if tok[0] == "c":
                        self.items[tok[1]][tok[2]]["inc"] = True
        vals = {}
        for e in STREAMS:
            n = 0
            for i, it in enumerate(self.items[e]):
                if it["inc"] and not it["dma"]:
                    n += 1
                    vals[("c", e, i)] = n
        ctx = ExitStack()
        csem, dsem = {}, {}
        for e in STREAMS:
            if any(it["inc"] and not it["dma"] for it in self.items[e]):
                csem[e] = ctx.enter_context(nc.semaphore(f"cs_{e}"))
        for skey in self.ndma:
            dsem[skey] = ctx.enter_context(nc.semaphore(f"ds_{skey}"))

        def wait_spec(tok):
            if tok[0] == "d":
                return ("d" + str(tok[1]), dsem[tok[1]], 16 * (tok[2] + 1))
            return ("c" + tok[1], csem[tok[1]], vals[tok])

        block = ctx.enter_context(nc.Block())

        def body(engname):
            def run(engine):
                hi = {}
                for it in self.items[engname]:
                    for tok in it["waits"]:
                        key, sem, v = wait_spec(tok)
                        if hi.get(key, 0) >= v:
                            continue
                        hi[key] = v
                        engine.wait_ge(sem, v)
                    if it["fn"] is None:
                        continue
                    inst = it["fn"](engine)
                    if it["dma"]:
                        inst.then_inc(dsem[it["dkey"]], 16)
                    elif it["inc"]:
                        inst.then_inc(csem[engname], 1)
            return run

        block.tensor(body("tensor"))
        block.vector(body("vector"))
        block.scalar(body("scalar"))
        block.gpsimd(body("gpsimd"))
        block.sync(body("sync"))
        ctx.close()



def build_program():
    nc = bass.Bass("TRN2", target_bir_lowering=False)
    p = Prog(nc)

    d_xp = nc.dram_tensor("xp", [NTT, 128, DM], F32, kind="ExternalInput").ap()
    d_kadj = nc.dram_tensor("kadj", [128, NTT], F32, kind="ExternalInput").ap()
    d_idb = nc.dram_tensor("idb", [128, 128], BF16, kind="ExternalInput").ap()
    d_idf = nc.dram_tensor("idf", [128, 128], F32, kind="ExternalInput").ap()
    slim = "slim" in SKIP
    def dram_in(name, shape, used=True):
        if slim and not used:
            shape = [1, 1]
        return nc.dram_tensor(name, shape, F32, kind="ExternalInput").ap()
    use_attn = "s1" not in SKIP
    use_s2 = "s2" not in SKIP
    use_s4 = "s4" not in SKIP
    d_wq = dram_in("wq_t", [H, 128, NDI * DH], use_attn)
    d_wk = dram_in("wk_t", [H, 128, NDI * DH], use_attn)
    d_wv = dram_in("wv_t", [H, 128, NDI * DH], use_attn)
    d_ow = dram_in("ow_t", [NDI, 128, DM], use_s2)
    d_rw = dram_in("rw_t", [NDI, 128, E])
    d_sg = dram_in("sg_t", [4, 128, NDI * 512])
    d_su = dram_in("su_t", [4, 128, NDI * 512])
    d_sd = dram_in("sd_t", [4, 128, 16 * 512])
    d_gw = dram_in("gw_t", [E, 2, 128, NDI * 512], use_s4)
    d_uw = dram_in("uw_t", [E, 2, 128, NDI * 512], use_s4)
    d_dw = dram_in("dw_t", [E, 2, 128, 8 * 1024], use_s4)
    d_out = nc.dram_tensor("out", [NOT, 128, DM], F32, kind="ExternalOutput").ap()
    dtap = {}
    for nm, shape in (("xnt", [128, NDI * TOK]), ("attnT", [128, H * TOK]),
                      ("po", [128, NOT * DM]), ("lt", [128, NOT * E]),
                      ("ge", [128, NOT * E]), ("acc", [128, NOT * DM]),
                      ("hsh", [128, 16 * TOK]), ("acc3", [128, NOT * DM]),
                      ("he0", [128, 8 * TOK]), ("m8w", [128, 16]),
                      ("gb0", [128, NDI * 512]), ("gb1", [128, NDI * 512]),
                      ("gss", [128, 4 * TOK]),
                      ("h0KT", [128, S]), ("h0QT", [128, TOK]),
                      ("h0Vh", [128, NTT * DH]), ("h0Pu", [128, NTT * 512]),
                      ("h0rb", [128, 512])):
        if ("h0" in DBG and nm.startswith("h0")) or nm in DBG:
            dtap[nm] = nc.dram_tensor("tap_" + nm, shape, F32, kind="ExternalOutput").ap()

    def tap(nm, ap_fn, reads):
        if nm in dtap:
            p.emit("gpsimd", lambda e, nm=nm: e.dma_start(out=dtap[nm], in_=ap_fn()),
                   reads=reads, dma=True)

    top = ExitStack()
    sb = lambda name, shape, dt: top.enter_context(nc.sbuf_tensor(name, shape, dt))

    idb = sb("sidb", [128, 128], BF16)
    idf = sb("sidf", [128, 128], F32)
    kadj = sb("skadj", [128, NTT], F32)
    qio = sb("sqio", [128, TOK], F32)
    onec = sb("sonec", [128, 1], BF16)
    oner = sb("soner", [1, 128], F32)
    xnT_own = sb("sxnT_own", [128, NDI * TOK], BF16)
    ACC = sb("sACC", [128, NOT * DM], F32)
    lt = sb("slt", [128, NOT * E], F32)
    ge = sb("sge", [128, NOT * E], F32)
    stat = sb("sstat", [128, 16], F32)
    cst = sb("scst", [128, 2], F32)

    if "init" not in SKIP:
        p.emit("sync", lambda e: e.dma_start(out=idb[:], in_=d_idb), writes=["idb"], dma=True)
        p.emit("sync", lambda e: e.dma_start(out=idf[:], in_=d_idf), writes=["idf"], dma=True)
        p.emit("sync", lambda e: e.dma_start(out=kadj[:], in_=d_kadj), writes=["kadj"], dma=True)
    if "init" not in SKIP:
        p.emit("vector", lambda e: e.memset(onec[:], 1.0), writes=["onec"])
        p.emit("vector", lambda e: e.memset(cst[:, 0:1], EPS), writes=["cst"])
        p.emit("vector", lambda e: e.memset(cst[:, 1:2], -SHIFT), writes=["cst"])
        p.emit("vector", lambda e: e.memset(oner[:], 1.0), writes=["oner"])
        p.emit("gpsimd", lambda e: e.iota(qio[:], pattern=[[1, TOK]], base=0,
                                          channel_multiplier=0,
                                          allow_small_or_imprecise_dtypes=True),
               writes=["qio"])

    with ExitStack() as scopeA:
        sba = lambda name, shape, dt: scopeA.enter_context(nc.sbuf_tensor(name, shape, dt))
        attnT = sba("sattnT", [128, H * TOK], BF16)
        scopeB = ExitStack()
        xnT_oth = scopeB.enter_context(nc.sbuf_tensor("sxnT_oth", [128, NDI * TOK], BF16))

        if "s0comp" in SKIP:
            d_xno = nc.dram_tensor("dbg_xnt_own", [128, NDI * TOK], F32, kind="ExternalInput").ap()
            d_xnt2 = nc.dram_tensor("dbg_xnt_oth", [128, NDI * TOK], F32, kind="ExternalInput").ap()
            p.emit("gpsimd", lambda e: e.dma_start(out=xnT_own[:], in_=d_xno),
                   writes=["xnT_own"], dma=True)
            p.emit("gpsimd", lambda e: e.dma_start(out=xnT_oth[:], in_=d_xnt2),
                   writes=["xnT_oth"], dma=True)
        with ExitStack() as s0:
            sb0 = lambda name, shape, dt: s0.enter_context(nc.sbuf_tensor(name, shape, dt))
            xt = [sb0(f"sxt{i}", [128, DM], F32) for i in range(3)]
            sqs = sb0("ssqs", [128, DM], BF16)
            xnt = [sb0(f"sxnt{i}", [128, DM], BF16) for i in range(2)]
            pst = [s0.enter_context(nc.psum_tensor(f"pst{i}", [128, 128], BF16)) for i in range(4)]

            for tt in range(NTT if "s0comp" not in SKIP else 0):
                xbuf = xt[tt % 3]
                kx = f"xt{tt % 3}"
                p.emit("sync", lambda e, tt=tt, xbuf=xbuf: e.dma_start(out=xbuf[:], in_=d_xp[tt]),
                       writes=[kx], dma=True)
                p.emit("vector", lambda e, xbuf=xbuf: e.reduce_sum(stat[:, 0:1], xbuf[:], axis=AX.X),
                       reads=[kx], writes=["st_sum"])
                p.emit("scalar", lambda e, xbuf=xbuf: e.activation(sqs[:], xbuf[:], AF.Square,
                                                                   accum_out=stat[:, 1:2]),
                       reads=[kx], writes=["st_sq", "sqs"])
                p.emit("vector", lambda e: e.tensor_scalar(stat[:, 2:3], stat[:, 0:1],
                                                           1.0 / DM, None, ALU.mult),
                       reads=["st_sum"], writes=["st_mean"])
                p.emit("vector", lambda e: e.tensor_tensor(stat[:, 3:4], stat[:, 2:3],
                                                           stat[:, 2:3], ALU.mult),
                       reads=["st_mean"], writes=["st_m2"])
                p.emit("vector", lambda e: e.tensor_scalar(stat[:, 4:5], stat[:, 1:2],
                                                           1.0 / DM, stat[:, 3:4],
                                                           ALU.mult, ALU.subtract),
                       reads=["st_sq", "st_m2"], writes=["st_var"])
                p.emit("scalar", lambda e: e.activation(stat[:, 5:6], stat[:, 4:5],
                                                        AF.Sqrt, bias=cst[:, 0:1]),
                       reads=["st_var", "cst"], writes=["st_std"])
                p.emit("vector", lambda e: e.reciprocal(stat[:, 6:7], stat[:, 5:6]),
                       reads=["st_std"], writes=["st_rstd"])
                xnb = xnt[tt % 2]
                kxn = f"xnt{tt % 2}"
                p.emit("vector", lambda e, xbuf=xbuf, xnb=xnb: e.tensor_scalar(
                    xnb[:], xbuf[:], stat[:, 2:3], stat[:, 6:7], ALU.subtract, ALU.mult),
                    reads=[kx, "st_mean", "st_rstd"], writes=[kxn])
                dst = xnT_own if tt < NOT else xnT_oth
                dkey = "xnT_own" if tt < NOT else "xnT_oth"
                co = (tt % NOT) * 128
                for dint in range(NDI):
                    pb = pst[dint % 4]
                    kpb = f"pst{dint % 4}"
                    p.emit("tensor", lambda e, xnb=xnb, dint=dint, pb=pb: e.transpose(
                        pb[:], xnb[:, dint * 128:(dint + 1) * 128], idb[:]),
                        reads=[kxn, "idb"], writes=[kpb])
                    p.emit("scalar", lambda e, pb=pb, dst=dst, dint=dint, co=co: e.copy(
                        dst[:, dint * TOK + co: dint * TOK + co + 128], pb[:]),
                        reads=[kpb], writes=[dkey])
        tap("xnt", lambda: xnT_own[:], ["xnT_own"])
        p.barrier()

        with ExitStack() as s1:
            sb1 = lambda name, shape, dt: s1.enter_context(nc.sbuf_tensor(name, shape, dt))
            wb = [sb1(f"swb{i}", [128, NDI * DH], BF16) for i in range(2)]
            KT = sb1("sKT", [128, S], BF16)
            QT = sb1("sQT", [128, TOK], BF16)
            VT = [sb1(f"sVT{i}", [128, 512], BF16) for i in range(2)]
            Vh = sb1("sVh", [128, NTT * DH], BF16)
            Pu = sb1("sPu", [128, NTT * 512], BF16)
            rb = sb1("srb", [128, 512], F32)
            rinv = sb1("srinv", [1, 512], F32)
            psA = s1.enter_context(nc.psum_tensor("psA", [128, 512], F32))
            psB = s1.enter_context(nc.psum_tensor("psB", [128, 512], F32))
            psC = s1.enter_context(nc.psum_tensor("psC", [128, 512], F32))
            psD = s1.enter_context(nc.psum_tensor("psD", [128, 512], F32))
            psDen = s1.enter_context(nc.psum_tensor("psDen", [128, 512], F32))
            psT = [s1.enter_context(nc.psum_tensor(f"psT{i}", [128, 128], BF16))
                   for i in range(2)]
            psE = s1.enter_context(nc.psum_tensor("psE", [128, 512], F32))

            chunks = []
            for h in range(H):
                chunks += [(d_wk, h), (d_wv, h), (d_wq, h)]

            def emit_load(i):
                dram, h = chunks[i]
                slot = i % 2
                p.emit("gpsimd", lambda e, b=wb[slot], dram=dram, h=h: e.dma_start(
                    out=b[:], in_=dram[h]), writes=[f"wb{slot}"], dma=True)

            if "s1" not in SKIP:
                emit_load(0); emit_load(1)
            for h in range(H if "s1" not in SKIP else 0):
                ks, vs, qs = (3 * h) % 2, (3 * h + 1) % 2, (3 * h + 2) % 2
                for grp in range(4):
                    src = xnT_own if grp < 2 else xnT_oth
                    skey = "xnT_own" if grp < 2 else "xnT_oth"
                    off = (grp % 2) * 512
                    for dint in range(NDI):
                        p.emit("tensor", lambda e, dint=dint, src=src, off=off, ks=ks: e.matmul(
                            psA[:], wb[ks][:, dint * DH:(dint + 1) * DH],
                            src[:, dint * TOK + off: dint * TOK + off + 512],
                            start=(dint == 0), stop=(dint == NDI - 1)),
                            reads=[f"wb{ks}", skey], writes=["psA"])
                    p.emit("scalar", lambda e, grp=grp: e.copy(
                        KT[:, grp * 512:(grp + 1) * 512], psA[:]),
                        reads=["psA"], writes=["KT"])
                if 3 * h + 2 < len(chunks):
                    emit_load(3 * h + 2)
                for grp in range(4):
                    src = xnT_own if grp < 2 else xnT_oth
                    skey = "xnT_own" if grp < 2 else "xnT_oth"
                    off = (grp % 2) * 512
                    for dint in range(NDI):
                        p.emit("tensor", lambda e, dint=dint, src=src, off=off, vs=vs: e.matmul(
                            psB[:], wb[vs][:, dint * DH:(dint + 1) * DH],
                            src[:, dint * TOK + off: dint * TOK + off + 512],
                            start=(dint == 0), stop=(dint == NDI - 1)),
                            reads=[f"wb{vs}", skey], writes=["psB"])
                    vslot = grp % 2
                    p.emit("scalar", lambda e, vslot=vslot: e.copy(VT[vslot][:], psB[:]),
                           reads=["psB"], writes=[f"VT{vslot}"])
                    for j in range(4):
                        kt = grp * 4 + j
                        pt = psT[j % 2]
                        kpt = f"psT{j % 2}"
                        p.emit("tensor", lambda e, j=j, pt=pt, vslot=vslot: e.transpose(
                            pt[:], VT[vslot][:, j * 128:(j + 1) * 128], idb[:]),
                            reads=[f"VT{vslot}", "idb"], writes=[kpt])
                        p.emit("scalar", lambda e, kt=kt, pt=pt: e.copy(
                            Vh[:, kt * DH:(kt + 1) * DH], pt[:]),
                            reads=[kpt], writes=["Vh"])
                if 3 * h + 3 < len(chunks):
                    emit_load(3 * h + 3)
                for grp in range(2):
                    off = grp * 512
                    for dint in range(NDI):
                        p.emit("tensor", lambda e, dint=dint, off=off, qs=qs: e.matmul(
                            psC[:], wb[qs][:, dint * DH:(dint + 1) * DH],
                            xnT_own[:, dint * TOK + off: dint * TOK + off + 512],
                            start=(dint == 0), stop=(dint == NDI - 1)),
                            reads=[f"wb{qs}", "xnT_own"], writes=["psC"])
                    p.emit("scalar", lambda e, off=off: e.copy(
                        QT[:, off:off + 512], psC[:]),
                        reads=["psC"], writes=["QT"])
                if 3 * h + 4 < len(chunks):
                    emit_load(3 * h + 4)

                for qg in range(2):
                    qoff = qg * 512
                    psS = [(psA, "psA"), (psB, "psB"), (psE, "psE")]
                    for kt in range(NTT):
                        pk, kps = psS[kt % 3]
                        p.emit("tensor", lambda e, kt=kt, pk=pk, qoff=qoff: e.matmul(
                            pk[:], KT[:, kt * 128:(kt + 1) * 128],
                            QT[:, qoff:qoff + 512], start=True, stop=True),
                            reads=["KT", "QT"], writes=[kps])
                        p.emit("scalar", lambda e, kt=kt, pk=pk: e.activation(
                            Pu[:, kt * 512:(kt + 1) * 512], pk[:], AF.Exp,
                            bias=cst[:, 1:2], scale=ISQ),
                            reads=[kps, "cst"], writes=[f"pu{kt}"])
                        p.emit("vector", lambda e, kt=kt, qoff=qoff: e.scalar_tensor_tensor(
                            Pu[:, kt * 512:(kt + 1) * 512],
                            qio[:, qoff:qoff + 512], kadj[:, kt:kt + 1],
                            Pu[:, kt * 512:(kt + 1) * 512],
                            ALU.is_ge, ALU.mult),
                            reads=[f"pu{kt}", "qio", "kadj"], writes=[f"pu{kt}"])
                    for kt in range(NTT):
                        p.emit("tensor", lambda e, kt=kt: e.matmul(
                            psDen[0:1, :], onec[:], Pu[:, kt * 512:(kt + 1) * 512],
                            start=(kt == 0), stop=(kt == NTT - 1)),
                            reads=[f"pu{kt}", "onec"], writes=["psDen"])
                    for kt in range(NTT):
                        p.emit("tensor", lambda e, kt=kt: e.matmul(
                            psC[:], Vh[:, kt * DH:(kt + 1) * DH],
                            Pu[:, kt * 512:(kt + 1) * 512],
                            start=(kt == 0), stop=(kt == NTT - 1)),
                            reads=[f"pu{kt}", "Vh"], writes=["psC"])
                    p.emit("vector", lambda e: e.reciprocal(rinv[:], psDen[0:1, :]),
                           reads=["psDen"], writes=["rinv"])
                    p.emit("tensor", lambda e: e.matmul(
                        psD[:], oner[:], rinv[:], start=True, stop=True),
                        reads=["rinv", "oner"], writes=["psD"])
                    p.emit("scalar", lambda e: e.copy(rb[:], psD[:]),
                           reads=["psD"], writes=["rb"])
                    p.emit("vector", lambda e, h=h, qoff=qoff: e.tensor_tensor(
                        attnT[:, h * TOK + qoff: h * TOK + qoff + 512],
                        psC[:], rb[:], ALU.mult),
                        reads=["psC", "rb"], writes=["attnT"])
                    if h == 0 and qg == 0 and "h0" in DBG:
                        tap("h0KT", lambda: KT[:], ["KT"])
                        tap("h0QT", lambda: QT[:], ["QT"])
                        tap("h0Vh", lambda: Vh[:], ["Vh"])
                        tap("h0Pu", lambda: Pu[:], ["pu%d" % k2 for k2 in range(NTT)])
                        tap("h0rb", lambda: rb[:], ["rb"])
        tap("attnT", lambda: attnT[:], ["attnT"])
        p.barrier()
        scopeB.close()

        with ExitStack() as s2:
            sb2 = lambda name, shape, dt: s2.enter_context(nc.sbuf_tensor(name, shape, dt))
            owb = sb2("sowb", [128, H * DM], BF16)
            psP = [s2.enter_context(nc.psum_tensor(f"psP{i}", [128, 512], F32))
                   for i in range(8)]
            if "s2" in SKIP and "noaccm" not in SKIP:
                p.emit("vector", lambda e: e.memset(ACC[:], 0.0), writes=["ACC"])
            for j in range(4 if "s2" not in SKIP else 0):
                p.emit("gpsimd", lambda e, j=j: e.dma_start(
                    out=owb[:, j * 4 * DM:(j + 1) * 4 * DM].rearrange(
                        "p (t d) -> p t d", t=4),
                    in_=d_ow.rearrange("t p d -> p t d")[:, j * 4:(j + 1) * 4, :]),
                    writes=[f"owb{j}"], dma=True)
            for tp in range(4 if "s2" not in SKIP else 0):
                for ft in range(H):
                    for t2 in range(2):
                        tt = tp * 2 + t2
                        for db in range(4):
                            pi = t2 * 4 + db
                            p.emit("tensor", lambda e, ft=ft, tt=tt, db=db, pi=pi: e.matmul(
                                psP[pi][:],
                                attnT[:, ft * TOK + tt * 128: ft * TOK + tt * 128 + 128],
                                owb[:, ft * DM + db * 512: ft * DM + db * 512 + 512],
                                start=(ft == 0), stop=(ft == H - 1)),
                                reads=["attnT", f"owb{ft // 4}"], writes=[f"psP{pi}"])
                for t2 in range(2):
                    tt = tp * 2 + t2
                    for db in range(4):
                        pi = t2 * 4 + db
                        p.emit("scalar", lambda e, tt=tt, db=db, pi=pi: e.copy(
                            ACC[:, tt * DM + db * 512: tt * DM + db * 512 + 512],
                            psP[pi][:]),
                            reads=[f"psP{pi}"], writes=["ACC"])
        tap("po", lambda: ACC[:], ["ACC"])
        p.barrier()

    with ExitStack() as s3:
        sb3 = lambda name, shape, dt: s3.enter_context(nc.sbuf_tensor(name, shape, dt))
        h_sh = sb3("sh_sh", [128, 16 * TOK], BF16)
        with ExitStack() as s3a:
            sb3a = lambda name, shape, dt: s3a.enter_context(nc.sbuf_tensor(name, shape, dt))
            rwb = sb3a("srwb", [128, NDI * E], BF16)
            ltT = sb3a("sltT", [16, TOK], F32)
            m8 = sb3a("sm8", [128, 8], F32)
            d12 = sb3a("sd12", [128, 2], F32)
            w12 = sb3a("sw12", [128, 2], F32)
            eqs = sb3a("seqs", [128, 2 * E], F32)
            gub = [sb3a(f"sgub{i}", [128, NDI * 512], BF16) for i in range(2)]
            gs = sb3a("sgs", [128, 4 * TOK], BF16)
            psg = [s3a.enter_context(nc.psum_tensor(f"psg{i}", [128, 512], F32))
                   for i in range(8)]

            if {"s3r", "s3g", "s3x1", "s3x2"} & SKIP:
                p.emit("vector", lambda e: e.memset(ge[:], 0.0625), writes=["ge"])
            else:
                p.emit("gpsimd", lambda e: e.dma_start(
                    out=rwb.rearrange("p (t e) -> p t e", t=NDI),
                    in_=d_rw.rearrange("t p e -> p t e")),
                    writes=["rwb"], dma=True)
            for tt in range(NOT if not ({"s3r", "s3x1"} & SKIP) else 0):
                for dint in range(NDI):
                    p.emit("tensor", lambda e, tt=tt, dint=dint, pt=psg: e.matmul(
                        pt[0][:, 0:E],
                        xnT_own[:, dint * TOK + tt * 128: dint * TOK + tt * 128 + 128],
                        rwb[:, dint * E:(dint + 1) * E],
                        start=(dint == 0), stop=(dint == NDI - 1)),
                        reads=["rwb", "xnT_own"], writes=["psg0"])
                p.emit("vector", lambda e, tt=tt, pt=psg: e.tensor_copy(
                    lt[:, tt * E:(tt + 1) * E], pt[0][:, 0:E]),
                    reads=["psg0"], writes=["lt"])
            for tt in range(NOT if not ({"s3r", "s3x1"} & SKIP) else 0):
                if "s3g" in SKIP:
                    continue
                p.emit("vector", lambda e, tt=tt: e.max(
                    out=m8[:], in_=lt[:, tt * E:(tt + 1) * E]),
                    reads=["lt"], writes=["m8"])
                p.emit("vector", lambda e: e.tensor_sub(d12[:, 0:1], m8[:, 0:1], m8[:, 1:2]),
                       reads=["m8"], writes=["d12"])
                p.emit("vector", lambda e: e.tensor_sub(d12[:, 1:2], m8[:, 1:2], m8[:, 0:1]),
                       reads=["m8"], writes=["d12"])
                p.emit("scalar", lambda e: e.activation(w12[:], d12[:], AF.Sigmoid),
                       reads=["d12"], writes=["w12"])
                p.emit("vector", lambda e, tt=tt: e.tensor_scalar(
                    eqs[:, 0:E], lt[:, tt * E:(tt + 1) * E], m8[:, 0:1], None, ALU.is_equal),
                    reads=["lt", "m8"], writes=["eq1"])
                p.emit("vector", lambda e, tt=tt: e.tensor_scalar(
                    eqs[:, E:2 * E], lt[:, tt * E:(tt + 1) * E], m8[:, 1:2], None, ALU.is_equal),
                    reads=["lt", "m8"], writes=["eq2"])
                p.emit("vector", lambda e: e.tensor_scalar(
                    eqs[:, 0:E], eqs[:, 0:E], w12[:, 0:1], None, ALU.mult),
                    reads=["eq1", "w12"], writes=["eq1"])
                p.emit("vector", lambda e: e.tensor_scalar(
                    eqs[:, E:2 * E], eqs[:, E:2 * E], w12[:, 1:2], None, ALU.mult),
                    reads=["eq2", "w12"], writes=["eq2"])
                p.emit("vector", lambda e, tt=tt: e.tensor_add(
                    ge[:, tt * E:(tt + 1) * E], eqs[:, 0:E], eqs[:, E:2 * E]),
                    reads=["eq1", "eq2"], writes=["ge"])
                if tt == 0 and "m8w" in dtap:
                    p.emit("vector", lambda e: e.tensor_copy(stat[:, 0:8], m8[:]),
                           reads=["m8"], writes=["m8snap"])
                    p.emit("vector", lambda e: e.tensor_copy(stat[:, 8:10], d12[:]),
                           reads=["d12"], writes=["m8snap"])
                    p.emit("vector", lambda e: e.tensor_copy(stat[:, 10:12], w12[:]),
                           reads=["w12"], writes=["m8snap"])
                    p.emit("vector", lambda e: e.tensor_copy(stat[:, 12:14], eqs[:, 0:2]),
                           reads=["eq1"], writes=["m8snap"])
                    tap("m8w", lambda: stat[:], ["m8snap"])

            def gu_block(d_g, d_u, nf, hbuf, hkey, gs_buf, gs_key, gukey, gub, psg):
                order = []
                for c in range(nf):
                    order += [("g", c), ("u", c)]

                def load(i):
                    kind, c = order[i]
                    dram = d_g if kind == "g" else d_u
                    slot = i % 2
                    p.emit("gpsimd", lambda e, b=gub[slot], dram=dram, c=c: e.dma_start(
                        out=b[:], in_=dram[c]), writes=[f"{gukey}{slot}"], dma=True)

                load(0); load(1)
                for i, (kind, c) in enumerate(order):
                    slot = i % 2
                    for fl in range(4):
                        for qg in range(2):
                            pi = fl * 2 + qg
                            for dint in range(NDI):
                                p.emit("tensor", lambda e, dint=dint, fl=fl, qg=qg, pi=pi, slot=slot,
                                       pt=psg, gb=gub: e.matmul(
                                    pt[pi][:],
                                    gb[slot][:, dint * 512 + fl * 128: dint * 512 + fl * 128 + 128],
                                    xnT_own[:, dint * TOK + qg * 512: dint * TOK + qg * 512 + 512],
                                    start=(dint == 0), stop=(dint == NDI - 1)),
                                    reads=[f"{gukey}{slot}", "xnT_own"], writes=[f"psg{pi}"])
                    if i + 2 < len(order):
                        load(i + 2)
                    if kind == "g":
                        for fl in range(4):
                            for qg in range(2):
                                pi = fl * 2 + qg
                                p.emit("scalar", lambda e, fl=fl, qg=qg, pi=pi, pt=psg: e.activation(
                                    gs_buf[:, fl * TOK + qg * 512: fl * TOK + qg * 512 + 512],
                                    pt[pi][:], AF.Silu),
                                    reads=[f"psg{pi}"], writes=[gs_key])
                    else:
                        for fl in range(4):
                            ft = c * 4 + fl
                            for qg in range(2):
                                pi = fl * 2 + qg
                                p.emit("vector", lambda e, fl=fl, ft=ft, qg=qg, pi=pi, pt=psg: e.tensor_tensor(
                                    hbuf[:, ft * TOK + qg * 512: ft * TOK + qg * 512 + 512],
                                    pt[pi][:],
                                    gs_buf[:, fl * TOK + qg * 512: fl * TOK + qg * 512 + 512],
                                    ALU.mult),
                                    reads=[f"psg{pi}", gs_key], writes=[hkey])

            import os as _os
            if _os.environ.get("K_E4"):
                p.barrier()
            if "s3sh" not in SKIP:
                gu_block(d_sg, d_su, 4, h_sh, "h_sh", gs, "gs", "gub", gub, psg)
                tap("gb0", lambda: gub[0][:], ["gub0"])
                tap("gb1", lambda: gub[1][:], ["gub1"])
                tap("gss", lambda: gs[:], ["gs"])
        p.barrier()
        with ExitStack() as s3b:
            sb3b = lambda name, shape, dt: s3b.enter_context(nc.sbuf_tensor(name, shape, dt))
            sdb = [sb3b(f"ssdb{i}", [128, 16 * 512], BF16) for i in range(2)]
            psq = [s3b.enter_context(nc.psum_tensor(f"psq{i}", [128, 512], F32))
                   for i in range(2)]
            for dq in range(4 if not ({"s3sh", "s3d"} & SKIP) else 0):
                slot = dq % 2
                p.emit("gpsimd", lambda e, b=sdb[slot], dq=dq: e.dma_start(
                    out=b[:], in_=d_sd[dq]), writes=[f"sdb{slot}"], dma=True)
                for tt in range(NOT):
                    pb = psq[tt % 2]
                    kpb = f"psq{tt % 2}"
                    for fd in range(16):
                        p.emit("tensor", lambda e, fd=fd, tt=tt, pb=pb, slot=slot: e.matmul(
                            pb[:], h_sh[:, fd * TOK + tt * 128: fd * TOK + tt * 128 + 128],
                            sdb[slot][:, fd * 512:(fd + 1) * 512],
                            start=(fd == 0), stop=(fd == 15)),
                            reads=["h_sh", f"sdb{slot}"], writes=[kpb])
                    p.emit("vector", lambda e, tt=tt, dq=dq, pb=pb: e.tensor_add(
                        ACC[:, tt * DM + dq * 512: tt * DM + dq * 512 + 512],
                        pb[:],
                        ACC[:, tt * DM + dq * 512: tt * DM + dq * 512 + 512]),
                        reads=[kpb, "ACC"], writes=["ACC"])
        tap("hsh", lambda: h_sh[:], ["h_sh"])
        tap("acc3", lambda: ACC[:], ["ACC"])
    p.barrier()

    with ExitStack() as s4:
        sb4 = lambda name, shape, dt: s4.enter_context(nc.sbuf_tensor(name, shape, dt))
        h_e = sb4("sh_e", [128, 8 * TOK], BF16)
        gub4 = [sb4(f"sgub4{i}", [128, NDI * 512], BF16) for i in range(2)]
        gs4 = sb4("sgs4", [128, 4 * TOK], BF16)
        dwb = [sb4(f"sdwb{i}", [128, 8 * 1024], BF16) for i in range(2)]
        psg4 = [s4.enter_context(nc.psum_tensor(f"psg4{i}", [128, 512], F32))
               for i in range(8)]

        for ex in range(E if "s4" not in SKIP else 0):
            order = [("g", 0), ("u", 0), ("g", 1), ("u", 1)]

            def load(i, ex=ex):
                kind, c = order[i]
                dram = d_gw if kind == "g" else d_uw
                slot = i % 2
                p.emit("gpsimd", lambda e, b=gub4[slot], dram=dram, ex=ex, c=c: e.dma_start(
                    out=b[:], in_=dram[ex, c]), writes=[f"gub4{slot}"], dma=True)

            load(0); load(1)
            for dh in range(2):
                p.emit("gpsimd", lambda e, b=dwb[dh], ex=ex, dh=dh: e.dma_start(
                    out=b[:], in_=d_dw[ex, dh]), writes=[f"dwb{dh}"], dma=True)
            for i, (kind, c) in enumerate(order):
                slot = i % 2
                for fl in range(4):
                    for qg in range(2):
                        pi = fl * 2 + qg
                        for dint in range(NDI):
                            p.emit("tensor", lambda e, dint=dint, fl=fl, qg=qg, pi=pi, slot=slot: e.matmul(
                                psg4[pi][:],
                                gub4[slot][:, dint * 512 + fl * 128: dint * 512 + fl * 128 + 128],
                                xnT_own[:, dint * TOK + qg * 512: dint * TOK + qg * 512 + 512],
                                start=(dint == 0), stop=(dint == NDI - 1)),
                                reads=[f"gub4{slot}", "xnT_own"], writes=[f"psg4{pi}"])
                if i + 2 < len(order):
                    load(i + 2)
                if kind == "g":
                    for fl in range(4):
                        for qg in range(2):
                            pi = fl * 2 + qg
                            p.emit("scalar", lambda e, fl=fl, qg=qg, pi=pi: e.activation(
                                gs4[:, fl * TOK + qg * 512: fl * TOK + qg * 512 + 512],
                                psg4[pi][:], AF.Silu),
                                reads=[f"psg4{pi}"], writes=["gs4"])
                else:
                    for fl in range(4):
                        ft = c * 4 + fl
                        for qg in range(2):
                            pi = fl * 2 + qg
                            p.emit("vector", lambda e, fl=fl, ft=ft, qg=qg, pi=pi: e.tensor_tensor(
                                h_e[:, ft * TOK + qg * 512: ft * TOK + qg * 512 + 512],
                                psg4[pi][:],
                                gs4[:, fl * TOK + qg * 512: fl * TOK + qg * 512 + 512],
                                ALU.mult),
                                reads=[f"psg4{pi}", "gs4"], writes=["h_e"])
            if ex == 0:
                tap("he0", lambda: h_e[:], ["h_e"])
            for dh in range(2):
                for tt in range(NOT):
                    for db in range(2):
                        pi = (tt % 4) * 2 + db
                        for fd in range(8):
                            p.emit("tensor", lambda e, fd=fd, tt=tt, db=db, pi=pi, dh=dh: e.matmul(
                                psg4[pi][:],
                                h_e[:, fd * TOK + tt * 128: fd * TOK + tt * 128 + 128],
                                dwb[dh][:, fd * 1024 + db * 512: fd * 1024 + db * 512 + 512],
                                start=(fd == 0), stop=(fd == 7)),
                                reads=["h_e", f"dwb{dh}"], writes=[f"psg4{pi}"])
                        do = dh * 1024 + db * 512
                        p.emit("vector", lambda e, tt=tt, do=do, pi=pi, ex=ex: e.scalar_tensor_tensor(
                            ACC[:, tt * DM + do: tt * DM + do + 512],
                            psg4[pi][:], ge[:, tt * E + ex: tt * E + ex + 1],
                            ACC[:, tt * DM + do: tt * DM + do + 512],
                            ALU.mult, ALU.add),
                            reads=[f"psg4{pi}", "ge", "ACC"], writes=["ACC"])
    tap("lt", lambda: lt[:], ["lt"])
    if "m8w" in dtap:
        pass
    tap("ge", lambda: ge[:], ["ge"])
    tap("acc", lambda: ACC[:], ["ACC"])
    p.barrier()

    with ExitStack() as s5:
        sb5 = lambda name, shape, dt: s5.enter_context(nc.sbuf_tensor(name, shape, dt))
        xb = [sb5(f"sxb5{i}", [128, DM], F32) for i in range(2)]
        ob = [sb5(f"sob5{i}", [128, DM], F32) for i in range(2)]
        for tt in range(NOT if "s5" not in SKIP else 0):
            slot = tt % 2
            p.emit("sync", lambda e, tt=tt, slot=slot: e.dma_start(
                out=xb[slot][:], in_=d_xp[tt]), writes=[f"xb5{slot}"], dma=True)
            p.emit("vector", lambda e, tt=tt, slot=slot: e.tensor_add(
                ob[slot][:], xb[slot][:], ACC[:, tt * DM:(tt + 1) * DM]),
                reads=[f"xb5{slot}", "ACC"], writes=[f"ob5{slot}"])
            p.emit("sync", lambda e, tt=tt, slot=slot: e.dma_start(
                out=d_out[tt], in_=ob[slot][:]), reads=[f"ob5{slot}"], dma=True)

    p.barrier()
    p.finalize()
    top.close()
    return nc



def _pretile(w):
    out = {}
    for nm, key in (("q", "wq_t"), ("k", "wk_t"), ("v", "wv_t")):
        a = w[nm].reshape(NDI, 128, H, DH).transpose(2, 1, 0, 3).reshape(H, 128, NDI * DH)
        out[key] = np.ascontiguousarray(a)
    out["ow_t"] = np.ascontiguousarray(w["o"].reshape(NDI, 128, DM))
    out["rw_t"] = np.ascontiguousarray(w["r"].T.reshape(NDI, 128, E))
    out["sg_t"] = np.ascontiguousarray(
        w["sg"].reshape(NDI, 128, 4, 512).transpose(2, 1, 0, 3).reshape(4, 128, NDI * 512))
    out["su_t"] = np.ascontiguousarray(
        w["su"].reshape(NDI, 128, 4, 512).transpose(2, 1, 0, 3).reshape(4, 128, NDI * 512))
    out["sd_t"] = np.ascontiguousarray(
        w["sd"].reshape(16, 128, 4, 512).transpose(2, 1, 0, 3).reshape(4, 128, 16 * 512))
    out["gw_t"] = np.ascontiguousarray(
        w["gw"].reshape(E, NDI, 128, 2, 512).transpose(0, 3, 2, 1, 4).reshape(E, 2, 128, NDI * 512))
    out["uw_t"] = np.ascontiguousarray(
        w["uw"].reshape(E, NDI, 128, 2, 512).transpose(0, 3, 2, 1, 4).reshape(E, 2, 128, NDI * 512))
    out["dw_t"] = np.ascontiguousarray(
        w["dw"].reshape(E, 8, 128, 2, 1024).transpose(0, 3, 2, 1, 4).reshape(E, 2, 128, 8 * 1024))
    return out


def kernel(L0_attn_hidden_states, L0_attn_q_wei, L0_attn_k_wei, L0_attn_v_wei,
           L0_attn_out_wei, L0_ffn_router_w, L0_ffn_gate_w, L0_ffn_up_w,
           L0_ffn_down_w, L0_ffn_shared_gate_w, L0_ffn_shared_up_w,
           L0_ffn_shared_down_w):
    x = np.asarray(L0_attn_hidden_states, dtype=np.float32)
    wt = _pretile({
        "q": np.asarray(L0_attn_q_wei, np.float32),
        "k": np.asarray(L0_attn_k_wei, np.float32),
        "v": np.asarray(L0_attn_v_wei, np.float32),
        "o": np.asarray(L0_attn_out_wei, np.float32),
        "r": np.asarray(L0_ffn_router_w, np.float32),
        "sg": np.asarray(L0_ffn_shared_gate_w, np.float32),
        "su": np.asarray(L0_ffn_shared_up_w, np.float32),
        "sd": np.asarray(L0_ffn_shared_down_w, np.float32),
        "gw": np.asarray(L0_ffn_gate_w, np.float32),
        "uw": np.asarray(L0_ffn_up_w, np.float32),
        "dw": np.asarray(L0_ffn_down_w, np.float32),
    })
    idb = np.eye(128, dtype=ml_dtypes.bfloat16)
    idf = np.eye(128, dtype=np.float32)

    in_maps = []
    for c in range(8):
        b, par = c // 2, c % 2
        if par == 0:
            perm = np.arange(S)
        else:
            perm = np.concatenate([np.arange(1024, 2048), np.arange(0, 1024)])
        xp = np.ascontiguousarray(x[b][perm]).reshape(NTT, 128, DM)
        own_pos = perm[:TOK]
        kadj = np.searchsorted(own_pos, perm).astype(np.float32)
        kadj_sb = np.ascontiguousarray(kadj.reshape(NTT, 128).T)
        m = {"xp": xp, "kadj": kadj_sb, "idb": idb, "idf": idf}
        if "slim" in SKIP:
            dummy = np.zeros((1, 1), np.float32)
            for nm, used in (("wq_t", "s1" not in SKIP), ("wk_t", "s1" not in SKIP),
                             ("wv_t", "s1" not in SKIP), ("ow_t", "s2" not in SKIP),
                             ("gw_t", "s4" not in SKIP), ("uw_t", "s4" not in SKIP),
                             ("dw_t", "s4" not in SKIP)):
                if not used:
                    m[nm] = dummy
        if "s0comp" in SKIP:
            xpf = xp.reshape(S, DM).astype(np.float64)
            mu = xpf.mean(axis=1, keepdims=True)
            va = xpf.var(axis=1, keepdims=True)
            xnf = ((xpf - mu) / np.sqrt(va + EPS)).astype(np.float32)
            xnt_o = np.ascontiguousarray(
                xnf[:TOK].T.reshape(NDI, 128, TOK).reshape(128 * NDI, TOK)).reshape(NDI, 128, TOK)
            m["dbg_xnt_own"] = np.ascontiguousarray(
                xnt_o.transpose(1, 0, 2).reshape(128, NDI * TOK))
            xnt_t = xnf[TOK:].T.reshape(NDI, 128, TOK)
            m["dbg_xnt_oth"] = np.ascontiguousarray(
                xnt_t.transpose(1, 0, 2).reshape(128, NDI * TOK))
        m.update({k: v for k, v in wt.items() if k not in m})
        in_maps.append(m)

    nc = build_program()
    res = run_bass_kernel_spmd(nc, in_maps[:NCORES], core_ids=list(range(NCORES)))
    if DBG:
        global LAST_RES
        LAST_RES = res

    out = np.empty((B, S, DM), dtype=np.float32)
    for c in range(NCORES):
        b, par = c // 2, c % 2
        o = np.asarray(res.results[c]["out"]).reshape(TOK, DM)
        if par == 0:
            out[b, 0:1024] = o
        else:
            out[b, 1024:2048] = o
    return out
